# version 05 (final)
# baseline (speedup 1.0000x reference)
import numpy as np
from contextlib import ExitStack

import concourse.bass as bass
import concourse.mybir as mybir
from concourse.bass_utils import run_bass_kernel_spmd
import concourse.library_config as library_config

F32 = mybir.dt.float32
BF16 = mybir.dt.bfloat16
AF = mybir.ActivationFunctionType
ALU = mybir.AluOpType
AX = mybir.AxisListType

NCORES = 8
HAL = 128
NH = 4
D = 1024
INNER = 4096
MAGIC = float(3 * 2.0 ** 22)
FLOOR_NOW = 0.05

T, V, S, G, SY = "T", "V", "S", "G", "SY"
DG, DSY = "DG", "DSY"
ENGINE_OF = {T: T, V: V, S: S, G: G, SY: SY, DG: G, DSY: SY}


class Prog:
    def __init__(self):
        self.items = {k: [] for k in [T, V, S, G, SY]}
        self.cnt = {k: 0 for k in [T, V, S, G, SY, DG, DSY]}
        self.last_w = {}
        self.readers = {}
        self.waited = {a: {b: 0 for b in self.cnt} for a in [T, V, S, G, SY]}

    def op(self, tag, fn, reads=(), writes=(), dma=False):
        eng = ENGINE_OF[tag]
        waits = []

        def need(dep_tag, val):
            if val <= 0:
                return
            if self.waited[eng][dep_tag] < val:
                waits.append((dep_tag, val))
                self.waited[eng][dep_tag] = val

        for r in reads:
            lw = self.last_w.get(r)
            if lw:
                need(lw[0], lw[1])
        for w in writes:
            lw = self.last_w.get(w)
            if lw:
                need(lw[0], lw[1])
            for dtag, cn in self.readers.get(w, {}).items():
                need(dtag, cn)
        if dma:
            prev = self.cnt[tag]
            if prev > 0 and self.waited[eng][tag] < prev:
                waits.append((tag, prev))
                self.waited[eng][tag] = prev
        inc = 16 if dma else 1
        self.cnt[tag] += inc
        self.items[eng].append((waits, fn, tag, inc))
        for r in reads:
            self.readers.setdefault(r, {})[tag] = self.cnt[tag]
        for w in writes:
            self.last_w[w] = (tag, self.cnt[tag])
            self.readers[w] = {}


class _StopBuild(Exception):
    pass


def build_nc(TOK=2048, debug=False, stop=99):
    NT = TOK // 128
    SPAN = TOK + HAL
    ST = min(256, TOK)
    NST = (SPAN + ST - 1) // ST
    HALF = TOK // 2
    NIC = INNER // 128
    NTILE_ST = ST // 128

    nc = bass.Bass("TRN2", target_bir_lowering=False, debug=False)

    def din(name, shape, dtype=F32):
        return nc.dram_tensor(name, list(shape), dtype, kind="ExternalInput").ap()

    hs_d = din("hs", (SPAN, D))
    W_d = {n: din(n, (D, D)) for n in ["Wq", "Wk", "Wv", "Wid", "Wo", "gW1"]}
    Wb_d = din("Wb", (128, 32))
    gW1s_d = din("gW1s", (16, D))
    gb1_d = din("gb1", (128, 8))
    gW2_d = din("gW2", (128, 32))
    cw_d = {n: din(n, (128, 32)) for n in ["cwq", "cwk", "cwv"]}
    firs_d = din("firs", (128, 24))
    firl_d = din("firl", (128, 504))
    smalls_d = din("smalls", (NH, 16))
    onw_d = din("onw", (128, 2))
    wg_d = din("wg", (INNER, D))
    wu_d = din("wu", (INNER, D))
    wd_d = din("wd", (D, INNER))
    ident_d = din("ident", (128, 128), BF16)
    ident32_d = din("ident32", (128, 128))
    ones_d = din("ones", (128, 128), BF16)
    mskSLn_d = din("mskSLn", (128, 128))
    mskSUn_d = din("mskSUn", (128, 128))
    mskIL_d = din("mskIL", (128, 128))
    sel_d = din("sel", (128, 8))

    out_d = nc.dram_tensor("out", [TOK, D], F32, kind="ExternalOutput").ap()
    cc_in = nc.dram_tensor("cc_in", [128, 4096], BF16, kind="Internal")
    cst_dram = nc.dram_tensor("cst_dram", [8, 16], F32, kind="Internal")
    row_dram = nc.dram_tensor("row_dram", [1, 1024], F32, kind="Internal")
    cc_out = nc.dram_tensor("cc_out", [NCORES, 128, 4096], BF16, kind="Internal")

    dbg = {}
    if debug:
        for nm, shape in [("q_dbg", (8, 128, SPAN)), ("k_dbg", ((NT + 1) * 128, D)),
                          ("v_dbg", (8, 128, SPAN)), ("mc_dbg", (128, 4096)),
                          ("sin_dbg", (128, 2048)), ("delta_dbg", (NT * 128, D)),
                          ("x1_dbg", (NT * 128, D)),
                          ("cols_dbg", (128, (NT + 1) * 16)),
                          ("ls_dbg", (8, 128, TOK)), ("ll_dbg", (8, 128, TOK)),
                          ("xq_dbg", (8, 128, TOK)),
                          ("x1r_dbg", (128, NT * D)),
                          ("lnffn_dbg", (NT * 128, D)),
                          ("hq_dbg", (128, NIC * HALF))]:
            dbg[nm] = nc.dram_tensor(nm, list(shape), F32, kind="ExternalOutput").ap()

    es = ExitStack()
    sb = lambda n, p, f, dt: es.enter_context(nc.sbuf_tensor(n, [p, f], dt))
    ps = lambda n, p, f, dt: es.enter_context(nc.psum_tensor(n, [p, f], dt))

    ident = sb("identS", 128, 128, BF16)
    ident32 = sb("ident32S", 128, 128, F32)
    ones_sb = sb("onesS", 128, 128, BF16)
    mskSL = sb("mskSLS", 128, 128, F32)
    mskSU = sb("mskSUS", 128, 128, F32)
    mskIL = sb("mskILS", 128, 128, F32)
    sel_sb = sb("selS", 128, 8, F32)
    onw_sb = sb("onwS", 128, 2, F32)
    sm_sb = sb("smS", NH, 16, F32)
    consts = sb("consts", 128, 80, F32)
    cols = sb("cols", 128, (NT + 1) * 16, F32)
    cw_sb = sb("cw_sb", 128, 96, F32)
    fs_sb = sb("fs_sb", 128, 24, F32)
    fl_sb = sb("fl_sb", 128, 504, F32)
    g1s_sb = sb("g1s_sb", 16, D, BF16)
    g2_sb = sb("g2_sb", 128, 32, BF16)
    gb1_sb = sb("gb1_sb", 128, 8, F32)
    wb_sb = sb("wb_sb", 128, 32, BF16)
    mcol = sb("mcol", 128, 16, F32)
    work = sb("work", 128, 1536, F32)
    wbf = sb("wbf", 128, 4096, BF16)
    wld = sb("wld", 128, 1024, F32)
    sc_sb = sb("sc_sb", 128, 64, F32)

    pA0 = ps("pA0", 128, 512, F32)
    pA1 = ps("pA1", 128, 512, F32)
    pB0 = ps("pB0", 128, 512, F32)
    pC = ps("pC", 128, 512, F32)
    pD = ps("pD", 128, 512, F32)
    pr = ps("pr", 128, 512, F32)
    ptA = ps("ptA", 128, 128, BF16)
    ptB = ps("ptB", 128, 128, BF16)

    def PT(i):
        return (ptA if i % 2 == 0 else ptB)[:, 0:128]

    P = Prog()

    def dmaop(qtag, dst, src, reads, writes):
        P.op(qtag, lambda e: e.dma_start(out=dst, in_=src), reads, writes, dma=True)

    def mm(dst, lhsT, rhs, reads, writes, start=True, stop=True):
        P.op(T, lambda e: e.matmul(dst, lhsT, rhs, start=start, stop=stop),
             reads, writes)

    def tr(dst, src, reads, writes, f32=False):
        p = src.shape[0]
        idt = ident32[0:p, 0:p] if f32 else ident[0:p, 0:p]
        rd = tuple(reads) + (("c_ident32",) if f32 else ("c_ident",))
        P.op(T, lambda e: e.transpose(dst, src, idt), rd, writes)

    def act(tag, dst, src, func, reads, writes, bias=None, scale=None):
        kw = {}
        if bias is not None:
            kw["bias"] = bias
        if scale is not None:
            kw["scale"] = scale
        P.op(tag, lambda e: e.activation(out=dst, in_=src, func=func, **kw),
             reads, writes)

    def tt(tag, dst, a, b, op, reads, writes):
        P.op(tag, lambda e: e.tensor_tensor(out=dst, in0=a, in1=b, op=op),
             reads, writes)

    def tsc(tag, dst, a, s1, op0, reads, writes, s2=None, op1=None):
        if op1 is not None:
            P.op(tag, lambda e: e.tensor_scalar(out=dst, in0=a, scalar1=s1,
                                                scalar2=s2, op0=op0, op1=op1),
                 reads, writes)
        else:
            P.op(tag, lambda e: e.tensor_scalar(out=dst, in0=a, scalar1=s1,
                                                scalar2=None, op0=op0),
                 reads, writes)

    def sttop(tag, dst, a, scal, b, op0, op1, reads, writes):
        P.op(tag, lambda e: e.scalar_tensor_tensor(out=dst, in0=a, scalar=scal,
                                                   in1=b, op0=op0, op1=op1),
             reads, writes)

    def red(tag, dst, src, op, reads, writes, absval=None):
        P.op(tag, lambda e: e.tensor_reduce(out=dst, in_=src, axis=AX.X, op=op,
                                            apply_absolute_value=absval),
             reads, writes)

    def cpy(tag, dst, src, reads, writes):
        P.op(tag, lambda e: e.tensor_copy(out=dst, in_=src), reads, writes)

    def mset(tag, dst, val, writes):
        P.op(tag, lambda e: e.memset(dst, val), (), writes)

    W_U = "w_u"
    W_SCR = "w_scr"
    W_COL = "w_col"
    U_AP = lambda: work[:, 96:352]
    SCR = lambda a, b: work[:, 352 + a: 352 + b]
    COL = lambda j: work[:, j: j + 1]

    def BFR(a, b):
        return wbf[:, a: a + (b - a)]

    def chk(k):
        if stop <= k:
            raise _StopBuild()

    try:
        for dst, src, nm in [(ident, ident_d, "c_ident"), (ident32, ident32_d, "c_ident32"),
                             (ones_sb, ones_d, "c_ones"), (mskSL, mskSLn_d, "c_mskSL"),
                             (mskSU, mskSUn_d, "c_mskSU"), (mskIL, mskIL_d, "c_mskIL"),
                             (sel_sb, sel_d, "c_sel"), (onw_sb, onw_d, "c_onw"),
                             (sm_sb, smalls_d, "c_sm")]:
            dmaop(DSY, dst[:, :], src, [], [nm])
        dmaop(DSY, work[0:16, 352:352 + D], gW1s_d, [], [W_SCR])
        cpy(V, g1s_sb[:, :], work[0:16, 352:352 + D], [W_SCR], ["c_g1s"])
        dmaop(DSY, gb1_sb[:, :], gb1_d, [], ["c_gb1"])
        dmaop(DSY, work[:, 352:384], gW2_d, [], [W_SCR])
        cpy(V, g2_sb[:, :], work[:, 352:384], [W_SCR], ["c_g2"])
        dmaop(DSY, work[:, 384:416], Wb_d, [], [W_SCR])
        cpy(V, wb_sb[:, :], work[:, 384:416], [W_SCR], ["c_wb"])
        for nm, src, dst, K, base in [("c_cwq", cw_d["cwq"], cw_sb, 4, 0),
                                      ("c_cwk", cw_d["cwk"], cw_sb, 4, 32),
                                      ("c_cwv", cw_d["cwv"], cw_sb, 4, 64),
                                      ("c_firs", firs_d, fs_sb, 3, 0),
                                      ("c_firl", firl_d, fl_sb, 63, 0)]:
            dmaop(DSY, dst[:, base:base + 8 * K], src, [], [nm])
        act(S, work[0:NH, 0:4], sm_sb[:, 0:4], AF.Exp, ["c_sm"], [W_COL], scale=-1.0)
        act(S, work[0:NH, 4:8], sm_sb[:, 4:8], AF.Sigmoid, ["c_sm"], [W_COL])
        tsc(V, work[0:NH, 4:8], work[0:NH, 4:8], FLOOR_NOW, ALU.mult, [W_COL], [W_COL])
        red(V, work[0:NH, 8:9], work[0:NH, 4:8], ALU.add, [W_COL], [W_COL])
        tsc(V, work[0:NH, 8:9], work[0:NH, 8:9], -1.0, ALU.mult, [W_COL], [W_COL],
            s2=1.0, op1=ALU.add)
        act(S, work[0:NH, 9:10], sm_sb[:, 8:9], AF.Sigmoid, ["c_sm"], [W_COL])
        tt(V, work[0:NH, 10:14], sm_sb[:, 10:14], work[0:NH, 0:4], ALU.mult,
           ["c_sm", W_COL], [W_COL])
        cpy(V, work[0:NH, 14:15], sm_sb[:, 9:10], ["c_sm"], [W_COL])
        dmaop(DG, cst_dram.ap()[0:4, 0:15], work[0:4, 0:15], [W_COL], ["cstd"])
        for h in range(NH):
            dmaop(DG, consts[:, h * 20: h * 20 + 15],
                  cst_dram.ap()[h: h + 1, 0:15].to_broadcast((128, 15)),
                  ["cstd"], ["c_cst"])

        mset(V, cols[:, :], 0.0, ["colinit"])
        mset(V, consts[:, 78:79], 1e-5, ["c_eps5"])
        mset(V, consts[:, 79:80], 1e-6, ["c_eps6"])

        chk(0)
        esB = ExitStack()
        delta_sb = esB.enter_context(nc.sbuf_tensor("delta_sb", [128, NT * D], BF16))

        def dtm_ap(c, f0, n):
            return delta_sb[:, c * D + f0: c * D + f0 + n]

        esA1 = ExitStack()
        esA2 = ExitStack()
        v_sb = esA1.enter_context(nc.sbuf_tensor("v_sb", [128, 8 * SPAN], BF16))
        q_sb = esA2.enter_context(nc.sbuf_tensor("q_sb", [128, 8 * SPAN], BF16))
        k_sb = esA2.enter_context(nc.sbuf_tensor("k_sb", [128, (NT + 1) * D], BF16))

        def qfm(f, t0, n):
            return q_sb[:, f * SPAN + t0: f * SPAN + t0 + n]

        def vfm(f, t0, n):
            return v_sb[:, f * SPAN + t0: f * SPAN + t0 + n]

        def ktm(c, f0, n):
            return k_sb[:, (c + 1) * D + f0: (c + 1) * D + f0 + n]

        def SMR(t_, h, piece):
            return t_[:, (h * 2 + piece) * 256: (h * 2 + piece) * 256 + 256]

        es1 = ExitStack()
        sb1 = lambda n, p, f, dt: es1.enter_context(nc.sbuf_tensor(n, [p, f], dt))
        wcur = sb1("wcur", 128, 8 * D, BF16)
        zx = sb1("zx", 128, 8 * (ST + 8), BF16)
        hsld = sb1("hsld", 128, D, F32)
        ln1t = sb1("ln1t", 128, D, BF16)
        lnf = sb1("lnf", 128, 8 * ST, BF16)
        kraw = sb1("kraw", 128, 8 * ST, BF16)
        cacc = sb1("cacc", 128, 2 * ST, BF16)
        sqw = cacc

        def emit_ln(g, lnf_dst, hs_buf, ln1_buf, ln1_name):
            t0 = g * ST
            nt = min(ST, SPAN - t0)
            ntile = nt // 128
            for i in range(ntile):
                hts = hs_buf[:, 0:D]
                hn = "hsld"
                chk(0.02)

                dmaop(DSY, hts, hs_d[t0 + i * 128: t0 + (i + 1) * 128, :], [hn], [hn])
                red(V, COL(16), hts, ALU.add, [hn], [W_COL])
                act(S, SCR(0, 512), hts[:, 0:512], AF.Square, [hn], [W_SCR])
                red(V, COL(17), SCR(0, 512), ALU.add, [W_SCR], [W_COL])
                act(S, SCR(0, 512), hts[:, 512:1024], AF.Square, [hn], [W_SCR])
                red(V, COL(18), SCR(0, 512), ALU.add, [W_SCR], [W_COL])
                tt(V, COL(17), COL(17), COL(18), ALU.add, [W_COL], [W_COL])
                tsc(V, COL(19), COL(16), 1.0 / D, ALU.mult, [W_COL], [W_COL])
                tsc(V, COL(20), COL(17), 1.0 / D, ALU.mult, [W_COL], [W_COL])
                tt(V, COL(21), COL(19), COL(19), ALU.mult, [W_COL], [W_COL])
                tt(V, COL(22), COL(20), COL(21), ALU.subtract, [W_COL], [W_COL])
                act(S, COL(23), COL(22), AF.Sqrt, [W_COL, "c_eps5"], [W_COL], bias=consts[:, 78:79])
                P.op(V, lambda e: e.reciprocal(out=COL(23), in_=COL(23)),
                     [W_COL], [W_COL])
                sttop(V, COL(24), COL(19), -1.0, COL(23), ALU.mult, ALU.mult,
                      [W_COL], [W_COL])
                chk(0.05)
                act(S, ln1_buf[:, :], hts, AF.Identity, [hn, W_COL], [ln1_name],
                    bias=COL(24), scale=COL(23))
                chk(0.08)
                for f in range(8):
                    chk(0.1 + f * 0.002)
                    tr(PT(f), ln1_buf[:, f * 128:(f + 1) * 128], [ln1_name],
                       ["pt%d" % (f % 2)])
                    cpy(V, lnf_dst[:, f * ST + i * 128: f * ST + (i + 1) * 128],
                        PT(f), ["pt%d" % (f % 2)], ["lnf%d" % f])
            return t0, nt, ntile

        zpsum = [pA0[:, :], pA1[:, :]]
        zpn = ["pA0", "pA1"]
        lnfn = ["lnf%d" % f for f in range(8)]

        def emit_proj_conv(which):
            wn = {"q": "Wq", "k": "Wk", "v": "Wv"}[which]
            cwbase = {"q": 0, "k": 32, "v": 64}[which]
            cwname = "c_cw" + which
            for kc in range(8):
                buf = wld[:, 0:D]
                bufn = "wld"
                dmaop(DSY, buf, W_d[wn][kc * 128:(kc + 1) * 128, :], [bufn], [bufn])
                cpy(V, wcur[:, kc * D:(kc + 1) * D], buf, [bufn], ["wcur%d" % kc, bufn])
            for f in range(8):
                mset(V, zx[:, f * (ST + 8): f * (ST + 8) + 8], 0.0, ["zx%d" % f])
            for g in range(NST):
                t0, nt, ntile = emit_ln(g, lnf, hsld, ln1t, "ln1t")
                if which == "q":
                    chk(0.2)
                for grp in range(4):
                    for m4 in range(2):
                        mc = grp * 2 + m4
                        for kc in range(8):
                            mm(zpsum[m4][:, 0:nt],
                               wcur[:, kc * D + mc * 128: kc * D + mc * 128 + 128],
                               lnf[:, kc * ST: kc * ST + nt],
                               ["wcur%d" % kc, lnfn[kc]], [zpn[m4]],
                               start=(kc == 0), stop=(kc == 7))
                        cpy(V, zx[:, mc * (ST + 8) + 8: mc * (ST + 8) + 8 + nt],
                            zpsum[m4][:, 0:nt], [zpn[m4]], ["zx%d" % mc])
                    for m4 in range(2):
                        mc = grp * 2 + m4
                        accap = cacc[:, m4 * ST: m4 * ST + nt]
                        accn = "cacc%d" % m4
                        zr = lambda off: zx[:, mc * (ST + 8) + off:
                                            mc * (ST + 8) + off + nt]
                        tsc(V, accap, zr(5),
                            cw_sb[:, cwbase + mc * 4: cwbase + mc * 4 + 1], ALU.mult,
                            ["zx%d" % mc, cwname], [accn])
                        for tap in range(1, 4):
                            sttop(V, accap, zr(5 + tap),
                                  cw_sb[:, cwbase + mc * 4 + tap:
                                        cwbase + mc * 4 + tap + 1],
                                  accap, ALU.mult, ALU.add, ["zx%d" % mc, accn], [accn])
                        if which == "q":
                            act(S, qfm(mc, t0, nt), accap, AF.Silu, [accn],
                                ["qs%d" % mc])
                        elif which == "v":
                            act(S, vfm(mc, t0, nt), accap, AF.Silu, [accn],
                                ["vs%d" % mc])
                        else:
                            act(S, kraw[:, mc * ST: mc * ST + nt], accap, AF.Silu,
                                [accn], ["kraw%d" % mc])
                        cpy(V, zx[:, mc * (ST + 8): mc * (ST + 8) + 8],
                            zx[:, mc * (ST + 8) + nt: mc * (ST + 8) + nt + 8],
                            ["zx%d" % mc], ["zx%d" % mc])
                if which == "q":
                    chk(0.4)
                    for kc in range(8):
                        mm(pr[0:4, 0:nt], wb_sb[:, kc * 4:(kc + 1) * 4],
                           lnf[:, kc * ST: kc * ST + nt], ["c_wb", lnfn[kc]], ["pr"],
                           start=(kc == 0), stop=(kc == 7))
                    act(S, work[0:4, 32:32 + nt], pr[0:4, 0:nt], AF.Sigmoid, ["pr"],
                        ["rws"])
                    chk(0.6)
                    for h in range(NH):
                        for piece in range(2):
                            f = h * 2 + piece
                            act(S, sqw[:, 0:nt], qfm(f, t0, nt), AF.Square,
                                ["qs%d" % f], ["cacc0"])
                            mm(pD[0:1, 0:nt], ones_sb[:, 0:1], sqw[:, 0:nt],
                               ["c_ones", "cacc0"], ["pD", "pD"],
                               start=(piece == 0), stop=(piece == 1))
                        cpy(V, work[0:1, 1024:1024 + nt], pD[0:1, 0:nt],
                            ["pD", "pD"], [W_SCR])
                        dmaop(DSY, work[4 + h:5 + h, 32:32 + nt],
                              work[0:1, 1024:1024 + nt], [W_SCR], ["rws"])
                    chk(0.8)
                    for i in range(ntile):
                        tr(pC[:, 0:8], work[0:8, 32 + i * 128: 32 + (i + 1) * 128],
                           ["rws"], ["pC"], f32=True)
                        tile = g * NTILE_ST + i
                        cpy(V, cols[:, tile * 16: tile * 16 + 4], pC[:, 0:4],
                            ["pC"], ["colb%d" % tile])
                        act(S, cols[:, tile * 16 + 4: tile * 16 + 8], pC[:, 4:8],
                            AF.Sqrt, ["pC", "c_eps6"], ["colq%d" % tile], bias=consts[:, 79:80])
                        P.op(V, lambda e, tile=tile: e.reciprocal(
                            out=cols[:, tile * 16 + 4: tile * 16 + 8],
                            in_=cols[:, tile * 16 + 4: tile * 16 + 8]),
                            ["colq%d" % tile], ["colq%d" % tile])
                if which == "k":
                    for h in range(NH):
                        for piece in range(2):
                            f = h * 2 + piece
                            act(S, sqw[:, 0:nt], kraw[:, f * ST: f * ST + nt],
                                AF.Square, ["kraw%d" % f], ["cacc0"])
                            mm(pD[0:1, 0:nt], ones_sb[:, 0:1], sqw[:, 0:nt],
                               ["c_ones", "cacc0"], ["pD", "pD"],
                               start=(piece == 0), stop=(piece == 1))
                        cpy(V, work[0:1, 1024:1024 + nt], pD[0:1, 0:nt],
                            ["pD", "pD"], [W_SCR])
                        dmaop(DSY, work[h:h + 1, 32:32 + nt],
                              work[0:1, 1024:1024 + nt], [W_SCR], ["rws"])
                    for i in range(ntile):
                        tile = g * NTILE_ST + i
                        tr(pC[:, 128:132],
                           work[0:4, 32 + i * 128: 32 + (i + 1) * 128],
                           ["rws"], ["pC"], f32=True)
                        act(S, cols[:, tile * 16 + 8: tile * 16 + 12],
                            pC[:, 128:132], AF.Sqrt, ["pC", "c_eps6"],
                            ["colk%d" % tile], bias=consts[:, 79:80])
                        P.op(V, lambda e, tile=tile: e.reciprocal(
                            out=cols[:, tile * 16 + 8: tile * 16 + 12],
                            in_=cols[:, tile * 16 + 8: tile * 16 + 12]),
                            ["colk%d" % tile], ["colk%d" % tile])
                        for f in range(8):
                            h = f // 2
                            tr(PT(f), kraw[:, f * ST + i * 128: f * ST + (i + 1) * 128],
                               ["kraw%d" % f], ["pt%d" % (f % 2)])
                            act(S, k_sb[:, tile * D + f * 128: tile * D + (f + 1) * 128],
                                PT(f), AF.Identity,
                                ["pt%d" % (f % 2), "colk%d" % tile], ["ktm%d" % tile],
                                scale=cols[:, tile * 16 + 8 + h: tile * 16 + 9 + h])

        emit_proj_conv("q")
        chk(1)
        emit_proj_conv("k")
        chk(2)
        emit_proj_conv("v")
        es1.close()
        chk(3)

        if debug:
            for f in range(8):
                dmaop(DG, dbg["q_dbg"][f], q_sb[:, f * SPAN:(f + 1) * SPAN],
                      ["qs%d" % f], [])
                dmaop(DG, dbg["v_dbg"][f], v_sb[:, f * SPAN:(f + 1) * SPAN],
                      ["vs%d" % f], [])
            for t in range(NT + 1):
                dmaop(DG, dbg["k_dbg"][t * 128:(t + 1) * 128, :],
                      k_sb[:, t * D:(t + 1) * D], ["ktm%d" % t], [])
            dmaop(DG, dbg["cols_dbg"], cols.ap(),
                  ["colb%d" % t for t in range(NT + 1)] +
                  ["colq%d" % t for t in range(NT + 1)] +
                  ["colk%d" % t for t in range(NT + 1)], [])

        esD1 = ExitStack()
        Tt_sb = esD1.enter_context(nc.sbuf_tensor("Tt_sb", [128, NT * NH * 128], BF16))
        S_sb = esD1.enter_context(nc.sbuf_tensor("S_sb", [128, NH * 512], F32))
        M_sb = esD1.enter_context(nc.sbuf_tensor("M_sb", [128, NH * 512], F32))
        esD2 = ExitStack()
        mcg = esD2.enter_context(nc.sbuf_tensor("mcg", [128, 4096], BF16))
        wmld = esD2.enter_context(nc.sbuf_tensor("wmld", [128, 1024], F32))

        def BFR(a, b):
            return wbf[:, a: b]

        def Tts(c, h):
            i = (c * NH + h) * 128
            return Tt_sb[:, i: i + 128]

        flat_w = {"wg": wg_d.rearrange("a b -> (a b)").rearrange("(a b) -> a b", b=D),
                  "wu": wu_d.rearrange("a b -> (a b)").rearrange("(a b) -> a b", b=D),
                  "wd": wd_d.rearrange("a b -> (a b)").rearrange("(a b) -> a b", b=D)}
        mean_jobs = [(nm, i) for nm in ["wg", "wu", "wd"] for i in range(NIC)]
        mean_ptr = [0]
        mset(V, mcol[:, 0:3], 0.0, ["mcolA"])

        def emit_mean_jobs(nj):
            for _ in range(nj):
                if mean_ptr[0] >= len(mean_jobs):
                    return
                nm, i = mean_jobs[mean_ptr[0]]
                mean_ptr[0] += 1
                buf = wmld[:, 0:D]
                bufn = "wm"

                col = {"wg": 0, "wu": 1, "wd": 2}[nm]
                dmaop(DG, buf, flat_w[nm][i * 128:(i + 1) * 128, :], [bufn], [bufn])
                red(V, mcol[:, 4 + col:5 + col], buf, ALU.add, [bufn], ["mred"],
                    absval=True)
                tt(V, mcol[:, col:col + 1], mcol[:, col:col + 1],
                   mcol[:, 4 + col:5 + col], ALU.add, ["mred", "mcolA"],
                   ["mcolA", bufn])

        def emit_uw(c, h, with_T):
            bcol = cols[:, (c + 1) * 16 + h: (c + 1) * 16 + h + 1]
            act(S, BFR(0, 256), ktm(c, h * 256, 256), AF.Identity,
                ["ktm%d" % (c + 1), "colb%d" % (c + 1)], ["r_kbtm"], scale=bcol)
            for piece in range(2):
                f = h * 2 + piece
                tr(PT(piece), vfm(f, HAL + c * 128, 128), ["vs%d" % f],
                   ["pt%d" % (piece % 2)])
                act(S, BFR(256 + piece * 128, 256 + (piece + 1) * 128), PT(piece),
                    AF.Identity, ["pt%d" % (piece % 2), "colb%d" % (c + 1)], ["r_vbtm"],
                    scale=bcol)
                tr(PT(2 + piece), BFR(piece * 128, (piece + 1) * 128), ["r_kbtm"],
                   ["pt%d" % ((2 + piece) % 2)])
                cpy(V, BFR(512 + piece * 128, 512 + (piece + 1) * 128), PT(2 + piece),
                    ["pt%d" % ((2 + piece) % 2)], ["r_kbfm"])
                tr(PT(4 + piece), ktm(c, h * 256 + piece * 128, 128),
                   ["ktm%d" % (c + 1)], ["pt%d" % ((4 + piece) % 2)])
                cpy(V, BFR(768 + piece * 128, 768 + (piece + 1) * 128), PT(4 + piece),
                    ["pt%d" % ((4 + piece) % 2)], ["r_kfm"])
            if with_T:
                for piece in range(2):
                    mm(pC[:, 0:128], BFR(512 + piece * 128, 512 + (piece + 1) * 128),
                       BFR(768 + piece * 128, 768 + (piece + 1) * 128),
                       ["r_kbfm", "r_kfm"], ["pC"],
                       start=(piece == 0), stop=(piece == 1))
                for piece in range(2):
                    mm(pC[:, 256:384], BFR(768 + piece * 128, 768 + (piece + 1) * 128),
                       BFR(512 + piece * 128, 512 + (piece + 1) * 128),
                       ["r_kbfm", "r_kfm"], ["pC"],
                       start=(piece == 0), stop=(piece == 1))
                tt(V, BFR(2048, 2176), pC[:, 0:128], mskSL[:, :], ALU.mult,
                   ["pC", "c_mskSL"], ["r_Asb"])
                tt(V, BFR(2176, 2304), pC[:, 256:384], mskSU[:, :], ALU.mult,
                   ["pC", "c_mskSU"], ["r_Atsb"])
                tt(V, Tts(c, h), BFR(2176, 2304), ident[:, :], ALU.add,
                   ["r_Atsb", "c_ident"], ["r_Tt"])
                cpy(V, BFR(1536, 1664), BFR(2048, 2176), ["r_Asb"], ["r_Ba"])
                cpy(V, BFR(1792, 1920), BFR(2176, 2304), ["r_Atsb"], ["r_Bta"])
                bcur, btcur = 1536, 1792
                bnm, btnm = "r_Ba", "r_Bta"
                m = 1
                while m < 128:
                    nxt = 1664 if bcur == 1536 else 1536
                    ntx = 1920 if btcur == 1792 else 1792
                    nnm = "r_Bb" if bcur == 1536 else "r_Ba"
                    ntnm = "r_Btb" if btcur == 1792 else "r_Bta"
                    mm(pC[:, 0:128], BFR(btcur, btcur + 128), BFR(bcur, bcur + 128),
                       [bnm, btnm], ["pC"])
                    cpy(V, BFR(nxt, nxt + 128), pC[:, 0:128], ["pC"], [nnm])
                    if m * 2 < 128:
                        mm(pC[:, 256:384], BFR(bcur, bcur + 128),
                           BFR(btcur, btcur + 128), [bnm, btnm], ["pC"])
                        cpy(V, BFR(ntx, ntx + 128), pC[:, 256:384], ["pC"], [ntnm])
                    mm(pD[:, 0:128], BFR(nxt, nxt + 128), Tts(c, h),
                       [nnm, "r_Tt"], ["pD"])
                    cpy(V, BFR(2304, 2432), pD[:, 0:128], ["pD"], ["r_tmpb"])
                    tt(V, Tts(c, h), Tts(c, h), BFR(2304, 2432), ALU.add,
                       ["r_tmpb", "r_Tt"], ["r_Tt"])
                    bcur, btcur = nxt, ntx
                    bnm, btnm = nnm, ntnm
                    m *= 2
            mm(pD[:, 256:512], Tts(c, h), BFR(256, 512), ["r_Tt", "r_vbtm"], ["pD"])
            cpy(V, U_AP(), pD[:, 256:512], ["pD"], [W_U])
            mm(pC[:, 0:256], Tts(c, h), BFR(0, 256), ["r_Tt", "r_kbtm"], ["pC"])
            cpy(V, BFR(1024, 1280), pC[:, 0:256], ["pC"], ["r_wtm"])
            for piece in range(2):
                tr(PT(6 + piece), BFR(1024 + piece * 128, 1024 + (piece + 1) * 128),
                   ["r_wtm"], ["pt%d" % ((6 + piece) % 2)])
                cpy(V, BFR(1280 + piece * 128, 1280 + (piece + 1) * 128),
                    PT(6 + piece), ["pt%d" % ((6 + piece) % 2)], ["r_wfm"])

        mset(V, S_sb[:, :], 0.0, ["Sm"])
        mset(V, M_sb[:, :], 0.0, ["Mm"])
        for h in range(NH):
            for piece in range(2):
                cpy(V, SMR(M_sb, h, piece)[:, piece * 128:(piece + 1) * 128],
                    ident32[:, :], ["c_ident32"], ["Mm"])
        for c in range(NT):
            for h in range(NH):
                emit_uw(c, h, with_T=True)
                cpy(V, BFR(2432, 2944), S_sb[:, h * 512: h * 512 + 512], ["Sm"],
                    ["r_sbf"])
                for piece in range(2):
                    mm(pC[:, 256:512], BFR(1280 + piece * 128, 1280 + (piece + 1) * 128),
                       BFR(2432 + piece * 256, 2432 + (piece + 1) * 256),
                       ["r_wfm", "r_sbf"], ["pC"],
                       start=(piece == 0), stop=(piece == 1))
                tt(V, BFR(2944, 3200), U_AP(), pC[:, 256:512], ALU.subtract,
                   [W_U, "pC"], ["r_uadj"])
                for piece in range(2):
                    pn = "pD" if piece == 0 else "pD"
                    mm(pD[:, piece * 256:(piece + 1) * 256],
                       ktm(c, h * 256 + piece * 128, 128), BFR(2944, 3200),
                       ["ktm%d" % (c + 1), "r_uadj"], [pn])
                    tt(V, SMR(S_sb, h, piece), SMR(S_sb, h, piece),
                       pD[:, piece * 256:(piece + 1) * 256], ALU.add, [pn, "Sm"],
                       ["Sm"])
                cpy(V, BFR(2432, 2944), M_sb[:, h * 512: h * 512 + 512], ["Mm"],
                    ["r_sbf"])
                for piece in range(2):
                    mm(pC[:, 256:512], BFR(1280 + piece * 128, 1280 + (piece + 1) * 128),
                       BFR(2432 + piece * 256, 2432 + (piece + 1) * 256),
                       ["r_wfm", "r_sbf"], ["pC"],
                       start=(piece == 0), stop=(piece == 1))
                cpy(V, BFR(3328, 3584), pC[:, 256:512], ["pC"], ["r_wmb"])
                for piece in range(2):
                    pn = "pD" if piece == 0 else "pD"
                    mm(pD[:, piece * 256:(piece + 1) * 256],
                       ktm(c, h * 256 + piece * 128, 128), BFR(3328, 3584),
                       ["ktm%d" % (c + 1), "r_wmb"], [pn])
                    tt(V, SMR(M_sb, h, piece), SMR(M_sb, h, piece),
                       pD[:, piece * 256:(piece + 1) * 256], ALU.subtract,
                       [pn, "Mm"], ["Mm"])
            emit_mean_jobs(6)
        emit_mean_jobs(len(mean_jobs))

        for h in range(NH):
            for pm in range(2):
                for pc2 in range(2):
                    cpy(V, BFR(2304, 2432),
                        SMR(M_sb, h, pm)[:, pc2 * 128:(pc2 + 1) * 128], ["Mm"],
                        ["r_tmpb"])
                    tr(PT(0), BFR(2304, 2432), ["r_tmpb"], ["pt0"])
                    cpy(V, mcg[:, h * 512 + pc2 * 256 + pm * 128:
                               h * 512 + pc2 * 256 + pm * 128 + 128],
                        PT(0), ["pt0"], ["mcg"])
            for pm in range(2):
                cpy(V, mcg[:, 2048 + h * 512 + pm * 256:
                           2048 + h * 512 + pm * 256 + 256],
                    SMR(S_sb, h, pm), ["Sm"], ["mcg"])
        chk(4)
        dmaop(DG, cc_in.ap(), mcg[:, :], ["mcg"], ["ccin"])
        if debug:
            dmaop(DG, dbg["mc_dbg"], mcg[:, :], ["mcg"], [])
        P.op(G, lambda e: e.collective_compute(
            "AllGather", ALU.bypass, ins=[cc_in.ap()], outs=[cc_out.ap()],
            replica_groups=[[0, 1, 2, 3, 4, 5, 6, 7]]), ["ccin"], ["ccout"])

        chk(5)
        R_sb = M_sb
        dmaop(DSY, mcg[:, :], cc_out[0], ["ccout", "mcg"], ["mcg"])
        for h in range(NH):
            for pm in range(2):
                cpy(V, SMR(R_sb, h, pm),
                    mcg[:, 2048 + h * 512 + pm * 256: 2048 + h * 512 + pm * 256 + 256],
                    ["mcg"], ["Rc", "Mm"])
        mset(V, S_sb[:, :], 0.0, ["Sm"])
        for h in range(NH):
            for pm in range(2):
                sttop(V, SMR(S_sb, h, pm), SMR(R_sb, h, pm), sel_sb[:, 0:1],
                      SMR(S_sb, h, pm), ALU.mult, ALU.add, ["Rc", "c_sel", "Sm"],
                      ["Sm"])
        cur, nxt = R_sb, R_sb
        for j in range(1, NCORES - 1):
            dmaop(DSY, mcg[:, :], cc_out[j], ["ccout", "mcg"], ["mcg"])
            for h in range(NH):
                cpy(V, BFR(2432, 2944), cur[:, h * 512: h * 512 + 512], ["Rc"],
                    ["r_sbf"])
                for po in range(2):
                    pn = "pC" if po == 0 else "pC"
                    for pi in range(2):
                        mm(pC[:, po * 256:(po + 1) * 256],
                           mcg[:, h * 512 + pi * 256 + po * 128:
                               h * 512 + pi * 256 + po * 128 + 128],
                           BFR(2432 + pi * 256, 2432 + (pi + 1) * 256),
                           ["mcg", "r_sbf"], [pn], start=(pi == 0), stop=(pi == 1))
                    cpy(V, work[:, 352 + po * 256: 352 + (po + 1) * 256],
                        mcg[:, 2048 + h * 512 + po * 256:
                            2048 + h * 512 + po * 256 + 256], ["mcg"], [W_SCR])
                    tt(V, SMR(nxt, h, po), pC[:, po * 256:(po + 1) * 256],
                       work[:, 352 + po * 256: 352 + (po + 1) * 256], ALU.add,
                       [pn, W_SCR], ["Rc"])
            for h in range(NH):
                for pm in range(2):
                    sttop(V, SMR(S_sb, h, pm), SMR(nxt, h, pm), sel_sb[:, j: j + 1],
                          SMR(S_sb, h, pm), ALU.mult, ALU.add, ["Rc", "c_sel", "Sm"],
                          ["Sm"])
        esD2.close()
        if debug:
            dmaop(DG, dbg["sin_dbg"], S_sb.ap(), ["Sm"], [])

        chk(6)
        for c in range(NT):
            for h in range(NH):
                emit_uw(c, h, with_T=False)
                cpy(V, BFR(2432, 2944), S_sb[:, h * 512: h * 512 + 512], ["Sm"],
                    ["r_sbf"])
                for piece in range(2):
                    mm(pC[:, 256:512], BFR(1280 + piece * 128, 1280 + (piece + 1) * 128),
                       BFR(2432 + piece * 256, 2432 + (piece + 1) * 256),
                       ["r_wfm", "r_sbf"], ["pC"],
                       start=(piece == 0), stop=(piece == 1))
                tt(V, BFR(2944, 3200), U_AP(), pC[:, 256:512], ALU.subtract,
                   [W_U, "pC"], ["r_uadj"])
                for piece in range(2):
                    mm(pC[:, 0:128], BFR(768 + piece * 128, 768 + (piece + 1) * 128),
                       qfm(h * 2 + piece, HAL + c * 128, 128),
                       ["r_kfm", "qs%d" % (h * 2 + piece)], ["pC"],
                       start=(piece == 0), stop=(piece == 1))
                tt(V, BFR(3200, 3328), pC[:, 0:128], mskIL[:, :], ALU.mult,
                   ["pC", "c_mskIL"], ["r_att"])
                for piece in range(2):
                    mm(pD[:, 0:256], qfm(h * 2 + piece, HAL + c * 128, 128),
                       BFR(2432 + piece * 256, 2432 + (piece + 1) * 256),
                       ["qs%d" % (h * 2 + piece), "r_sbf"], ["pD"],
                       start=(piece == 0), stop=False)
                mm(pD[:, 0:256], BFR(3200, 3328), BFR(2944, 3200),
                   ["r_att", "r_uadj"], ["pD"], start=False, stop=True)
                act(S, dtm_ap(c, h * 256, 256), pD[:, 0:256], AF.Identity,
                    ["pD", "colq%d" % (c + 1)], ["dtm%d" % c],
                    scale=cols[:, (c + 1) * 16 + 4 + h: (c + 1) * 16 + 5 + h])
                for piece in range(2):
                    pn = "pC" if piece == 0 else "pC"
                    mm(pC[:, piece * 256:(piece + 1) * 256],
                       ktm(c, h * 256 + piece * 128, 128), BFR(2944, 3200),
                       ["ktm%d" % (c + 1), "r_uadj"], [pn])
                    tt(V, SMR(S_sb, h, piece), SMR(S_sb, h, piece),
                       pC[:, piece * 256:(piece + 1) * 256], ALU.add, [pn, "Sm"],
                       ["Sm"])
        if debug:
            for t in range(NT):
                dmaop(DG, dbg["delta_dbg"][t * 128:(t + 1) * 128, :],
                      delta_sb[:, t * D:(t + 1) * D], ["dtm%d" % t], [])
        esD1.close()
        esA2.close()

        chk(7)
        x1_dram = nc.dram_tensor("x1_dram", [128, NT * D], BF16, kind="Internal")
        es2 = ExitStack()
        sb2 = lambda n, p, f, dt: es2.enter_context(nc.sbuf_tensor(n, [p, f], dt))
        x1_sb = sb2("x1_sb", 128, NT * D, BF16)
        wid_sb = sb2("wid_sb", 128, 8 * D, BF16)
        g1_sb = sb2("g1_sb", 128, 8 * D, BF16)
        wo_sb = sb2("wo_sb", 128, 8 * D, BF16)
        lsf = sb2("lsf", 128, 8 * ST, BF16)
        llf = sb2("llf", 128, 8 * ST, BF16)
        hsld2 = sb2("hsld2", 128, D, F32)
        ln1t2 = sb2("ln1t2", 128, D, BF16)
        lnf2 = sb2("lnf2", 128, 8 * 128, BF16)
        stt_sb = sb2("stt_sb", 16, 128, BF16)
        gacc = sb2("gacc", 128, ST, BF16)

        for wn, dstw, wpre in [("Wid", wid_sb, "wid"), ("gW1", g1_sb, "g1w"),
                               ("Wo", wo_sb, "wow")]:
            for kc in range(8):
                buf = wld[:, 0:D]
                bufn = "wld"

                dmaop(DSY, buf, W_d[wn][kc * 128:(kc + 1) * 128, :], [bufn], [bufn])
                cpy(V, dstw[:, kc * D:(kc + 1) * D], buf, [bufn],
                    [wpre + "%d" % kc, bufn])

        lnf2n = ["lnf2_%d" % f for f in range(8)]
        NG2 = TOK // ST
        for g in range(NG2):
            t0s = HAL + g * ST
            for f in range(8):
                tsc(V, lsf[:, f * ST: f * ST + ST], vfm(f, t0s - 2, ST),
                    fs_sb[:, f * 3: f * 3 + 1], ALU.mult,
                    ["vs%d" % f, "c_firs"], ["lsf%d" % f])
                for tap in range(1, 3):
                    sttop(V, lsf[:, f * ST: f * ST + ST], vfm(f, t0s - 2 + tap, ST),
                          fs_sb[:, f * 3 + tap: f * 3 + tap + 1],
                          lsf[:, f * ST: f * ST + ST], ALU.mult, ALU.add,
                          ["vs%d" % f, "c_firs", "lsf%d" % f], ["lsf%d" % f])
                tsc(V, llf[:, f * ST: f * ST + ST], vfm(f, t0s - 62, ST),
                    fl_sb[:, f * 63: f * 63 + 1], ALU.mult,
                    ["vs%d" % f, "c_firl"], ["llf%d" % f])
                for tap in range(1, 63):
                    sttop(V, llf[:, f * ST: f * ST + ST],
                          vfm(f, t0s - 62 + tap, ST),
                          fl_sb[:, f * 63 + tap: f * 63 + tap + 1],
                          llf[:, f * ST: f * ST + ST], ALU.mult, ALU.add,
                          ["vs%d" % f, "c_firl", "llf%d" % f], ["llf%d" % f])
            for il in range(NTILE_ST):
                c = g * NTILE_ST + il
                dmaop(DSY, hsld2[:, :], hs_d[HAL + c * 128: HAL + (c + 1) * 128, :],
                      ["hs2"], ["hs2"])
                red(V, COL(16), hsld2[:, :], ALU.add, ["hs2"], [W_COL])
                act(S, SCR(0, 512), hsld2[:, 0:512], AF.Square, ["hs2"], [W_SCR])
                red(V, COL(17), SCR(0, 512), ALU.add, [W_SCR], [W_COL])
                act(S, SCR(0, 512), hsld2[:, 512:1024], AF.Square, ["hs2"], [W_SCR])
                red(V, COL(18), SCR(0, 512), ALU.add, [W_SCR], [W_COL])
                tt(V, COL(17), COL(17), COL(18), ALU.add, [W_COL], [W_COL])
                tsc(V, COL(19), COL(16), 1.0 / D, ALU.mult, [W_COL], [W_COL])
                tsc(V, COL(20), COL(17), 1.0 / D, ALU.mult, [W_COL], [W_COL])
                tt(V, COL(21), COL(19), COL(19), ALU.mult, [W_COL], [W_COL])
                tt(V, COL(22), COL(20), COL(21), ALU.subtract, [W_COL], [W_COL])
                act(S, COL(23), COL(22), AF.Sqrt, [W_COL, "c_eps5"], [W_COL], bias=consts[:, 78:79])
                P.op(V, lambda e: e.reciprocal(out=COL(23), in_=COL(23)),
                     [W_COL], [W_COL])
                sttop(V, COL(24), COL(19), -1.0, COL(23), ALU.mult, ALU.mult,
                      [W_COL], [W_COL])
                act(S, ln1t2[:, :], hsld2[:, :], AF.Identity, ["hs2", W_COL],
                    ["ln1t2"], bias=COL(24), scale=COL(23))
                for f in range(8):
                    tr(PT(f), ln1t2[:, f * 128:(f + 1) * 128], ["ln1t2"],
                       ["pt%d" % (f % 2)])
                    cpy(V, lnf2[:, f * 128:(f + 1) * 128], PT(f), ["pt%d" % (f % 2)],
                        [lnf2n[f]])
                for mc in range(8):
                    pdst = (pA0 if mc < 4 else pB0)[:, (mc % 4) * 128:(mc % 4) * 128 + 128]
                    pnm = "pA0" if mc < 4 else "pB0"

                    for kc in range(8):
                        mm(pdst, g1_sb[:, kc * D + mc * 128: kc * D + mc * 128 + 128],
                           lnf2[:, kc * 128:(kc + 1) * 128],
                           ["g1w%d" % kc, lnf2n[kc]], [pnm],
                           start=(kc == 0), stop=(kc == 7))
                    act(S, wbf[:, mc * 128:(mc + 1) * 128], pdst, AF.Identity,
                        [pnm, "c_gb1"], ["r_hsh"], bias=gb1_sb[:, mc: mc + 1])
                for h in range(NH):
                    pdst = pC[:, 0:256] if h % 2 == 0 else pC[:, 256:512]
                    pnm = "pC" if h % 2 == 0 else "pC"
                    for kc in range(8):
                        mm(pdst, lnf2[:, kc * 128:(kc + 1) * 128],
                           wid_sb[:, kc * D + h * 256: kc * D + h * 256 + 256],
                           ["wid%d" % kc, lnf2n[kc]], [pnm],
                           start=(kc == 0), stop=(kc == 7))
                    cpy(V, wbf[:, 1024 + h * 256: 1024 + (h + 1) * 256], pdst,
                        [pnm], ["r_idv"])
                for h in range(NH):
                    for piece in range(2):
                        f = h * 2 + piece
                        tr(PT(piece), vfm(f, HAL + c * 128, 128), ["vs%d" % f],
                           ["pt%d" % (piece % 2)])
                        cpy(V, wbf[:, 2048 + piece * 128: 2048 + (piece + 1) * 128],
                            PT(piece), ["pt%d" % (piece % 2)], ["r_vtm"])
                        tr(PT(2 + piece),
                           lsf[:, f * ST + il * 128: f * ST + (il + 1) * 128],
                           ["lsf%d" % f], ["pt%d" % ((2 + piece) % 2)])
                        cpy(V, wbf[:, 2304 + piece * 128: 2304 + (piece + 1) * 128],
                            PT(2 + piece), ["pt%d" % ((2 + piece) % 2)], ["r_lstm"])
                        tr(PT(4 + piece),
                           llf[:, f * ST + il * 128: f * ST + (il + 1) * 128],
                           ["llf%d" % f], ["pt%d" % ((4 + piece) % 2)])
                        cpy(V, wbf[:, 2560 + piece * 128: 2560 + (piece + 1) * 128],
                            PT(4 + piece), ["pt%d" % ((4 + piece) % 2)], ["r_lltm"])
                    vtm = wbf[:, 2048:2304]
                    lstm = wbf[:, 2304:2560]
                    lltm = wbf[:, 2560:2816]
                    dslice = dtm_ap(c, h * 256, 256)
                    idtm = wbf[:, 1024 + h * 256: 1024 + (h + 1) * 256]
                    for bi, (br, brn) in enumerate([(lstm, "r_lstm"), (lltm, "r_lltm"),
                                                    (dslice, "dtm%d" % c),
                                                    (vtm, "r_vtm")]):
                        j0 = bi * 4
                        red(V, COL(32 + j0), br, ALU.add, [brn], [W_COL])
                        act(S, SCR(0, 256), br, AF.Square, [brn], [W_SCR])
                        red(V, COL(33 + j0), SCR(0, 256), ALU.add, [W_SCR], [W_COL])
                        red(V, COL(34 + j0), br, ALU.add, [brn], [W_COL], absval=True)
                    for bi in range(4):
                        j0 = bi * 4
                        tsc(V, COL(64 + j0), COL(32 + j0), 1.0 / 256, ALU.mult,
                            [W_COL], [W_COL])
                        tsc(V, COL(66 + j0), COL(34 + j0), 1.0 / 256, ALU.mult,
                            [W_COL], [W_COL])
                        act(S, COL(67 + j0), COL(33 + j0), AF.Sqrt, [W_COL], [W_COL])
                        tsc(V, COL(48 + j0), COL(33 + j0), 1.0 / 256, ALU.mult,
                            [W_COL], [W_COL])
                        tt(V, COL(49 + j0), COL(64 + j0), COL(64 + j0), ALU.mult,
                           [W_COL], [W_COL])
                        tt(V, COL(65 + j0), COL(48 + j0), COL(49 + j0), ALU.subtract,
                           [W_COL], [W_COL])
                    tr(pD[0:16, 0:128], work[:, 64:80], [W_COL], ["pD"], f32=True)
                    cpy(V, stt_sb[0:16, 0:128], pD[0:16, 0:128], ["pD"], ["sttb"])
                    for mc in range(8):
                        pdst = (pA1 if mc < 4 else pr)[
                            :, (mc % 4) * 128:(mc % 4) * 128 + 128]
                        pnm = "pA1" if mc < 4 else "pr"

                        mm(pdst, g1s_sb[:, mc * 128:(mc + 1) * 128],
                           stt_sb[0:16, 0:128], ["c_g1s", "sttb"], [pnm])
                        cpy(V, wbf[:, 2816 + mc * 128: 2816 + (mc + 1) * 128], pdst,
                            [pnm], ["r_hidg"])
                        tt(V, wbf[:, 2816 + mc * 128: 2816 + (mc + 1) * 128],
                           wbf[:, 2816 + mc * 128: 2816 + (mc + 1) * 128],
                           wbf[:, mc * 128:(mc + 1) * 128], ALU.add,
                           ["r_hidg", "r_hsh"], ["r_hidg"])
                        act(S, wbf[:, 2816 + mc * 128: 2816 + (mc + 1) * 128],
                            wbf[:, 2816 + mc * 128: 2816 + (mc + 1) * 128], AF.Gelu,
                            ["r_hidg"], ["r_hidg"])
                    for kc in range(8):
                        mm(pr[0:4, 0:128], g2_sb[:, kc * 4:(kc + 1) * 4],
                           wbf[:, 2816 + kc * 128: 2816 + (kc + 1) * 128],
                           ["c_g2", "r_hidg"], ["pr"],
                           start=(kc == 0), stop=(kc == 7))
                    cpy(V, work[0:4, 96:224], pr[0:4, 0:128], ["pr"], [W_COL, W_U])
                    tr(pD[0:128, 128:132], work[0:4, 96:224], [W_COL, W_U], ["pD"],
                       f32=True)
                    cpy(V, work[:, 81:85], pD[:, 128:132], ["pD"], [W_COL])
                    tt(V, work[:, 81:85], work[:, 81:85],
                       consts[:, h * 20: h * 20 + 4], ALU.mult, [W_COL, "c_cst"],
                       [W_COL])
                    tt(V, work[:, 81:85], work[:, 81:85],
                       consts[:, h * 20 + 10: h * 20 + 14], ALU.add,
                       [W_COL, "c_cst"], [W_COL])
                    red(V, COL(27), work[:, 81:85], ALU.max, [W_COL], [W_COL])
                    tsc(V, work[:, 81:85], work[:, 81:85], COL(27), ALU.subtract,
                        [W_COL], [W_COL])
                    act(S, work[:, 81:85], work[:, 81:85], AF.Exp, [W_COL], [W_COL])
                    red(V, COL(27), work[:, 81:85], ALU.add, [W_COL], [W_COL])
                    P.op(V, lambda e: e.reciprocal(out=COL(28), in_=COL(27)),
                         [W_COL], [W_COL])
                    tsc(V, work[:, 81:85], work[:, 81:85], COL(28), ALU.mult,
                        [W_COL], [W_COL])
                    sttop(V, work[:, 81:85], work[:, 81:85],
                          consts[:, h * 20 + 8: h * 20 + 9],
                          consts[:, h * 20 + 4: h * 20 + 8], ALU.mult, ALU.add,
                          [W_COL, "c_cst"], [W_COL])
                    tt(V, COL(29), work[:, 81:82], consts[:, h * 20 + 9: h * 20 + 10],
                       ALU.add, [W_COL, "c_cst"], [W_COL])
                    o_ap = work[:, 500:756]
                    tsc(V, o_ap, lstm, COL(29), ALU.mult, ["r_lstm", W_COL], [W_SCR])
                    sttop(V, o_ap, lltm, work[:, 82:83], o_ap, ALU.mult, ALU.add,
                          ["r_lltm", W_COL, W_SCR], [W_SCR])
                    sttop(V, o_ap, dslice, work[:, 83:84], o_ap, ALU.mult, ALU.add,
                          ["dtm%d" % c, W_COL, W_SCR], [W_SCR])
                    sttop(V, o_ap, vtm, work[:, 84:85], o_ap, ALU.mult, ALU.add,
                          ["r_vtm", W_COL, W_SCR], [W_SCR])
                    sttop(V, o_ap, idtm, consts[:, h * 20 + 14: h * 20 + 15], o_ap,
                          ALU.mult, ALU.add, ["r_idv", "c_cst", W_SCR], [W_SCR])
                    act(S, work[:, 756:1012], o_ap, AF.Square, [W_SCR], [W_SCR])
                    red(V, COL(30), work[:, 756:1012], ALU.add, [W_SCR], [W_COL])
                    act(S, COL(31), COL(30), AF.Sqrt, [W_COL, "c_eps5"], [W_COL],
                        bias=consts[:, 78:79], scale=1.0 / 256)
                    P.op(V, lambda e: e.reciprocal(out=COL(31), in_=COL(31)),
                         [W_COL], [W_COL])
                    tsc(V, wbf[:, 3840:4096], o_ap, COL(31), ALU.mult,
                        [W_SCR, W_COL], ["r_obf"])
                    for piece in range(2):
                        tr(PT(6 + piece), wbf[:, 3840 + piece * 128:
                                              3840 + (piece + 1) * 128],
                           ["r_obf"], ["pt%d" % ((6 + piece) % 2)])
                        act(S, wbf[:, 1024 + h * 256 + piece * 128:
                                   1024 + h * 256 + (piece + 1) * 128],
                            PT(6 + piece), AF.Identity,
                            ["pt%d" % ((6 + piece) % 2), "c_onw"], ["r_idv"],
                            scale=onw_sb[:, piece: piece + 1])
                for mc in range(8):
                    pdst = (pA0 if mc < 4 else pB0)[:, (mc % 4) * 128:(mc % 4) * 128 + 128]
                    pnm = "pA0" if mc < 4 else "pB0"

                    for kc in range(8):
                        mm(pdst, wo_sb[:, kc * D + mc * 128: kc * D + mc * 128 + 128],
                           wbf[:, 1024 + kc * 128: 1024 + (kc + 1) * 128],
                           ["wow%d" % kc, "r_idv"], [pnm],
                           start=(kc == 0), stop=(kc == 7))
                    cpy(V, wbf[:, mc * 128:(mc + 1) * 128], pdst, [pnm], ["r_hsh"])
                for f in range(8):
                    tr(PT(f), wbf[:, f * 128:(f + 1) * 128], ["r_hsh"], ["pt%d" % (f % 2)])
                    cpy(V, wbf[:, 2816 + f * 128: 2816 + (f + 1) * 128], PT(f),
                        ["pt%d" % (f % 2)], ["r_hidg"])
                attm = wbf[:, 2816:3840]
                red(V, COL(16), attm, ALU.add, ["r_hidg"], [W_COL])
                act(S, SCR(0, 1024), attm, AF.Square, ["r_hidg"], [W_SCR])
                red(V, COL(17), SCR(0, 512), ALU.add, [W_SCR], [W_COL])
                red(V, COL(18), SCR(512, 1024), ALU.add, [W_SCR], [W_COL])
                tt(V, COL(17), COL(17), COL(18), ALU.add, [W_COL], [W_COL])
                tsc(V, COL(19), COL(16), 1.0 / D, ALU.mult, [W_COL], [W_COL])
                tsc(V, COL(20), COL(17), 1.0 / D, ALU.mult, [W_COL], [W_COL])
                tt(V, COL(21), COL(19), COL(19), ALU.mult, [W_COL], [W_COL])
                tt(V, COL(22), COL(20), COL(21), ALU.subtract, [W_COL], [W_COL])
                act(S, COL(23), COL(22), AF.Sqrt, [W_COL, "c_eps5"], [W_COL], bias=consts[:, 78:79])
                P.op(V, lambda e: e.reciprocal(out=COL(23), in_=COL(23)),
                     [W_COL], [W_COL])
                sttop(V, COL(24), COL(19), -1.0, COL(23), ALU.mult, ALU.mult,
                      [W_COL], [W_COL])
                act(S, SCR(0, 1024), attm, AF.Identity, ["r_hidg", W_COL], [W_SCR],
                    bias=COL(24), scale=COL(23))
                tt(V, x1_sb[:, c * D:(c + 1) * D], hsld2[:, :], SCR(0, 1024),
                   ALU.add, ["hs2", W_SCR], ["x1_%d" % c])
        if debug:
            for t in range(NT):
                dmaop(DG, dbg["x1_dbg"][t * 128:(t + 1) * 128, :],
                      x1_sb[:, t * D:(t + 1) * D], ["x1_%d" % t], [])
            if ST == TOK:
                for f in range(8):
                    dmaop(DG, dbg["ls_dbg"][f], lsf[:, f * ST:(f + 1) * ST],
                          ["lsf%d" % f], [])
                    dmaop(DG, dbg["ll_dbg"][f], llf[:, f * ST:(f + 1) * ST],
                          ["llf%d" % f], [])
        dmaop(DG, x1_dram.ap(), x1_sb[:, :],
              ["x1_%d" % c for c in range(NT)], ["x1d"])
        es2.close()
        esA1.close()
        esB.close()

        chk(8)
        esF = ExitStack()
        sbF = lambda n, p, f, dt: esF.enter_context(nc.sbuf_tensor(n, [p, f], dt))
        x1_sb = sbF("x1_sb2", 128, NT * D, BF16)
        dmaop(DSY, x1_sb[:, :], x1_dram.ap(), ["x1d"], ["x1r"])
        if debug:
            dmaop(DG, dbg["x1r_dbg"], x1_sb[:, :], ["x1r"], [])
        xq_sb = sbF("xq_sb", 128, 8 * TOK, BF16)
        h_sb = sbF("h_sb", 128, NIC * HALF, BF16)
        wqT = sbF("wqT", 128, 8 * 128, BF16)
        wdT_sb = sbF("wdT_sb", 128, NIC * 128, BF16)
        dtm2 = sbF("dtm2", 128, 8 * D, BF16)
        ln2w = sbF("ln2w", 128, D, F32)
        bxz = sbF("bxz", 128, 3 * HALF, F32)

        def xqfm(f, t0, n):
            return xq_sb[:, f * TOK + t0: f * TOK + t0 + n]

        tr(pD[0:3, 0:128], mcol[:, 0:3], ["mcolA"], ["pD"], f32=True)
        cpy(V, work[0:3, 352:480], pD[0:3, 0:128], ["pD"], [W_SCR])
        red(V, work[0:3, 480:481], work[0:3, 352:480], ALU.add, [W_SCR], [W_SCR])
        tsc(V, work[0:3, 481:482], work[0:3, 480:481], 1.0 / (INNER * D), ALU.mult,
            [W_SCR], [W_SCR])
        tsc(V, work[0:3, 482:483], work[0:3, 481:482], 1e-5, ALU.max, [W_SCR], [W_SCR])
        P.op(V, lambda e: e.reciprocal(out=work[0:3, 483:484],
                                       in_=work[0:3, 482:483]), [W_SCR], [W_SCR])
        P.op(DG, lambda e: e.dma_start(out=cst_dram.ap()[4:7, 0:1],
                                       in_=work[0:3, 483:484],
                                       allow_slow_non_contiguous=True),
             [W_SCR], ["cstd2"], dma=True)
        P.op(DG, lambda e: e.dma_start(out=cst_dram.ap()[4:7, 1:2],
                                       in_=work[0:3, 481:482],
                                       allow_slow_non_contiguous=True),
             [W_SCR], ["cstd2"], dma=True)
        for i in range(3):
            dmaop(DG, mcol[:, 8 + i: 9 + i],
                  cst_dram.ap()[4 + i: 5 + i, 0:1].to_broadcast((128, 1)),
                  ["cstd2"], ["c_wsc"])
            dmaop(DG, mcol[:, 11 + i: 12 + i],
                  cst_dram.ap()[4 + i: 5 + i, 1:2].to_broadcast((128, 1)),
                  ["cstd2"], ["c_wsc"])

        for c in range(NT):
            x1t = x1_sb[:, c * D:(c + 1) * D]
            x1n = "x1r"
            red(V, COL(16), x1t, ALU.add, [x1n], [W_COL])
            act(S, SCR(0, 512), x1t[:, 0:512], AF.Square, [x1n], [W_SCR])
            red(V, COL(17), SCR(0, 512), ALU.add, [W_SCR], [W_COL])
            act(S, SCR(0, 512), x1t[:, 512:1024], AF.Square, [x1n], [W_SCR])
            red(V, COL(18), SCR(0, 512), ALU.add, [W_SCR], [W_COL])
            tt(V, COL(17), COL(17), COL(18), ALU.add, [W_COL], [W_COL])
            tsc(V, COL(19), COL(16), 1.0 / D, ALU.mult, [W_COL], [W_COL])
            tsc(V, COL(20), COL(17), 1.0 / D, ALU.mult, [W_COL], [W_COL])
            tt(V, COL(21), COL(19), COL(19), ALU.mult, [W_COL], [W_COL])
            tt(V, COL(22), COL(20), COL(21), ALU.subtract, [W_COL], [W_COL])
            act(S, COL(23), COL(22), AF.Sqrt, [W_COL, "c_eps5"], [W_COL], bias=consts[:, 78:79])
            P.op(V, lambda e: e.reciprocal(out=COL(23), in_=COL(23)),
                 [W_COL], [W_COL])
            sttop(V, COL(24), COL(19), -1.0, COL(23), ALU.mult, ALU.mult,
                  [W_COL], [W_COL])
            act(S, ln2w[:, :], x1t, AF.Identity, [x1n, W_COL], ["ln2w"],
                bias=COL(24), scale=COL(23))
            red(V, COL(25), ln2w[:, :], ALU.max, ["ln2w"], [W_COL], absval=True)
            tsc(V, COL(25), COL(25), 1e-5, ALU.max, [W_COL], [W_COL])
            tsc(V, COL(26), COL(25), 1.0 / 127.0, ALU.mult, [W_COL], [W_COL])
            P.op(V, lambda e: e.reciprocal(out=COL(26), in_=COL(26)),
                 [W_COL], [W_COL])
            tsc(V, sc_sb[:, c: c + 1], COL(25), 1.0 / 127.0, ALU.mult, [W_COL],
                ["sxc%d" % c])
            tsc(V, SCR(0, 1024), ln2w[:, :], COL(26), ALU.mult, ["ln2w", W_COL],
                [W_SCR])
            tsc(V, ln2w[:, :], SCR(0, 1024), MAGIC, ALU.add, [W_SCR], ["ln2w"],
                s2=-MAGIC, op1=ALU.add)
            cpy(V, wbf[:, 0:1024], ln2w[:, :], ["ln2w"], ["r_hsh"])
            for f in range(8):
                tr(PT(f), wbf[:, f * 128:(f + 1) * 128], ["r_hsh"], ["pt%d" % (f % 2)])
                cpy(V, xqfm(f, c * 128, 128), PT(f), ["pt%d" % (f % 2)], ["xq%d" % f])
        if debug:
            for f in range(8):
                dmaop(DG, dbg["xq_dbg"][f], xq_sb[:, f * TOK:(f + 1) * TOK],
                      ["xq%d" % f], [])

        for Hh in range(2):
            ht0 = Hh * HALF
            ntile_h = HALF // 128
            for i in range(ntile_h):
                cti = Hh * ntile_h + i
                tr(pD[0:1, 0:128], sc_sb[:, cti: cti + 1], ["sxc%d" % cti], ["pD"],
                   f32=True)
                cpy(V, work[0:1, 352 + i * 128: 352 + (i + 1) * 128],
                    pD[0:1, 0:128], ["pD"], [W_SCR])
            dmaop(DG, row_dram.ap()[0:1, 0:HALF], work[0:1, 352:352 + HALF],
                  [W_SCR], ["rowd"])
            dmaop(DG, bxz[:, 0:HALF],
                  row_dram.ap()[0:1, 0:HALF].to_broadcast((128, HALF)),
                  ["rowd"], ["r_bxz"])
            tsc(V, bxz[:, HALF:2 * HALF], bxz[:, 0:HALF], mcol[:, 11:12], ALU.mult,
                ["r_bxz", "c_wsc"], ["r_bxg"])
            tsc(V, bxz[:, 2 * HALF:3 * HALF], bxz[:, 0:HALF], mcol[:, 12:13],
                ALU.mult, ["r_bxz", "c_wsc"], ["r_bxu"])
            for ic in range(NIC):
                for half2 in range(2):
                    nm = "wg" if half2 == 0 else "wu"
                    src = wg_d if half2 == 0 else wu_d
                    buf = wld[:, 0:D]
                    bufn = "wld"

                    dmaop(DSY, buf, src[ic * 128:(ic + 1) * 128, :], [bufn], [bufn])
                    act(S, SCR(0, 1024), buf, AF.Identity, [bufn, "c_wsc"], [W_SCR],
                        scale=mcol[:, 8 + half2: 9 + half2])
                    tsc(V, work[:, 352:352 + 1024] if False else SCR(0, 1024),
                        SCR(0, 1024), MAGIC, ALU.add, [W_SCR], [W_SCR],
                        s2=-MAGIC, op1=ALU.add)
                    tsc(V, wbf[:, 0:1024], SCR(0, 1024), 1.0, ALU.min, [W_SCR],
                        ["r_hsh", bufn], s2=-1.0, op1=ALU.max)
                    for f in range(8):
                        tr(PT(f), wbf[:, f * 128:(f + 1) * 128], ["r_hsh"],
                           ["pt%d" % (f % 2)])
                        cpy(V, wqT[:, f * 128:(f + 1) * 128], PT(f), ["pt%d" % (f % 2)],
                            ["r_wqT"])
                    ptens = [pA0, pA1] if half2 == 0 else [pB0, pr]
                    pnames = ["pA0", "pA1"] if half2 == 0 else ["pB0", "pr"]
                    NW = min(512, HALF)
                    for nhalf in range(HALF // NW):
                        for kc in range(8):
                            mm(ptens[nhalf][:, 0:NW],
                               wqT[:, kc * 128:(kc + 1) * 128],
                               xqfm(kc, ht0 + nhalf * NW, NW),
                               ["r_wqT", "xq%d" % kc], [pnames[nhalf]],
                               start=(kc == 0), stop=(kc == 7))
                NW2 = min(512, HALF)
                for nhalf in range(HALF // NW2):
                    tt(V, SCR(nhalf * NW2, nhalf * NW2 + NW2),
                       [pA0, pA1][nhalf][:, 0:NW2],
                       bxz[:, HALF + nhalf * NW2: HALF + (nhalf + 1) * NW2],
                       ALU.mult, [["pA0", "pA1"][nhalf], "r_bxg"], [W_SCR])
                act(S, wbf[:, 0:HALF], SCR(0, HALF), AF.Silu, [W_SCR], ["r_hsh"])
                for nhalf in range(HALF // NW2):
                    tt(V, SCR(nhalf * NW2, nhalf * NW2 + NW2),
                       [pB0, pr][nhalf][:, 0:NW2],
                       bxz[:, 2 * HALF + nhalf * NW2: 2 * HALF + (nhalf + 1) * NW2],
                       ALU.mult, [["pB0", "pr"][nhalf], "r_bxu"], [W_SCR])
                cpy(V, wbf[:, 1024:1024 + HALF], SCR(0, HALF), [W_SCR], ["r_idv"])
                tt(V, h_sb[:, ic * HALF:(ic + 1) * HALF], wbf[:, 0:HALF],
                   wbf[:, 1024:1024 + HALF], ALU.mult, ["r_hsh", "r_idv"], ["hic%d" % ic])
            act(S, wbf[:, 1024:1024 + HALF], h_sb[:, 0:HALF], AF.Abs,
                ["hic0"], ["r_idv"])
            cpy(V, wbf[:, 0:HALF], wbf[:, 1024:1024 + HALF], ["r_idv"], ["r_hsh"])
            for ic in range(1, NIC):
                act(S, wbf[:, 1024:1024 + HALF], h_sb[:, ic * HALF:(ic + 1) * HALF],
                    AF.Abs, ["hic%d" % ic], ["r_idv"])
                tt(V, wbf[:, 0:HALF], wbf[:, 0:HALF], wbf[:, 1024:1024 + HALF],
                   ALU.max, ["r_hsh", "r_idv"], ["r_hsh"])
            for b in range(HALF // 128):
                tr(PT(b), wbf[:, b * 128:(b + 1) * 128], ["r_hsh"], ["pt%d" % (b % 2)])
                cpy(V, wbf[:, 2048 + b * 128: 2048 + (b + 1) * 128], PT(b),
                    ["pt%d" % (b % 2)], ["r_vtm"])
                red(V, COL(32 + b), wbf[:, 2048 + b * 128: 2048 + (b + 1) * 128],
                    ALU.max, ["r_vtm"], [W_COL], absval=True)
            for b in range(HALF // 128):
                tsc(V, COL(40 + b), COL(32 + b), 1e-5, ALU.max, [W_COL], [W_COL])
                tsc(V, COL(48 + b), COL(40 + b), 1.0 / 127.0, ALU.mult,
                    [W_COL], [W_COL])
                P.op(V, lambda e, b=b: e.reciprocal(out=COL(48 + b),
                                                    in_=COL(48 + b)),
                     [W_COL], [W_COL])
                tsc(V, sc_sb[:, 32 + b: 33 + b], COL(40 + b), 1.0 / 127.0, ALU.mult,
                    [W_COL], ["shc%d" % b])
                tr(pD[0:1, 0:128], work[:, 48 + b: 49 + b], [W_COL], ["pD"],
                   f32=True)
                cpy(V, work[0:1, 352 + b * 128: 352 + (b + 1) * 128],
                    pD[0:1, 0:128], ["pD"], [W_SCR])
            dmaop(DG, row_dram.ap()[0:1, 0:HALF], work[0:1, 352:352 + HALF],
                  [W_SCR], ["rowd"])
            dmaop(DG, bxz[:, 0:HALF],
                  row_dram.ap()[0:1, 0:HALF].to_broadcast((128, HALF)),
                  ["rowd"], ["r_bxz"])
            for ic in range(NIC):
                tt(V, SCR(0, HALF), h_sb[:, ic * HALF:(ic + 1) * HALF],
                   bxz[:, 0:HALF], ALU.mult, ["hic%d" % ic, "r_bxz"], [W_SCR])
                tsc(V, h_sb[:, ic * HALF:(ic + 1) * HALF], SCR(0, HALF), MAGIC,
                    ALU.add, [W_SCR], ["hic%d" % ic], s2=-MAGIC, op1=ALU.add)
            if debug and Hh == 0:
                dmaop(DG, dbg["hq_dbg"], h_sb.ap(),
                      ["hic%d" % ic for ic in range(NIC)], [])
            for oc in range(8):
                for qp in range(4):
                    buf = wld[:, 0:D]
                    bufn = "wld"

                    dmaop(DSY, buf, wd_d[oc * 128:(oc + 1) * 128,
                                         qp * 1024:(qp + 1) * 1024], [bufn], [bufn])
                    act(S, SCR(0, 1024), buf, AF.Identity, [bufn, "c_wsc"], [W_SCR],
                        scale=mcol[:, 10:11])
                    tsc(V, SCR(0, 1024), SCR(0, 1024), MAGIC, ALU.add, [W_SCR],
                        [W_SCR], s2=-MAGIC, op1=ALU.add)
                    tsc(V, wbf[:, 0:1024], SCR(0, 1024), 1.0, ALU.min,
                        [W_SCR], ["r_hsh", bufn], s2=-1.0, op1=ALU.max)
                    for f in range(4):
                        ic = qp * 4 + f * 1 if False else qp * 4 + f
                        pass
                    for f in range(8):
                        tr(PT(f), wbf[:, f * 128:(f + 1) * 128], ["r_hsh"],
                           ["pt%d" % (f % 2)])
                        ici = qp * 8 + f
                        cpy(V, wdT_sb[:, ici * 128:(ici + 1) * 128], PT(f),
                            ["pt%d" % (f % 2)], ["r_wdT"])
                ptens = [pA0, pA1] if oc % 2 == 0 else [pB0, pr]
                pnames = ["pA0", "pA1"] if oc % 2 == 0 else ["pB0", "pr"]
                NW = min(512, HALF)
                for nhalf in range(HALF // NW):
                    for ic in range(NIC):
                        mm(ptens[nhalf][:, 0:NW],
                           wdT_sb[:, ic * 128:(ic + 1) * 128],
                           h_sb[:, ic * HALF + nhalf * NW:
                                ic * HALF + nhalf * NW + NW],
                           ["r_wdT", "hic%d" % ic], [pnames[nhalf]],
                           start=(ic == 0), stop=(ic == NIC - 1))
                for nhalf in range(HALF // NW):
                    cpy(V, wbf[:, nhalf * NW: (nhalf + 1) * NW],
                        ptens[nhalf][:, 0:NW], [pnames[nhalf]], ["r_hsh"])
                for b in range(HALF // 128):
                    tr(PT(b), wbf[:, b * 128:(b + 1) * 128], ["r_hsh"],
                       ["pt%d" % (b % 2)])
                    cpy(V, dtm2[:, b * D + oc * 128: b * D + (oc + 1) * 128],
                        PT(b), ["pt%d" % (b % 2)], ["dt2_%d" % b])
            for b in range(HALF // 128):
                c = Hh * (HALF // 128) + b
                dslice2 = dtm2[:, b * D:(b + 1) * D]
                tt(V, COL(16), sc_sb[:, 32 + b: 33 + b], mcol[:, 13:14], ALU.mult,
                   ["shc%d" % b, "c_wsc"], [W_COL])
                act(S, SCR(0, 1024), dslice2, AF.Identity,
                    ["dt2_%d" % b, W_COL], [W_SCR], scale=COL(16))
                red(V, COL(17), SCR(0, 1024), ALU.add, [W_SCR], [W_COL])
                act(S, work[:, 96:608] if False else SCR(0, 512),
                    SCR(0, 512), AF.Square, [W_SCR], [W_SCR]) if False else None
                act(S, work[:, 96:352], SCR(0, 256), AF.Square, [W_SCR], [W_U])
                red(V, COL(18), work[:, 96:352], ALU.add, [W_U], [W_COL])
                act(S, work[:, 96:352], SCR(256, 512), AF.Square, [W_SCR], [W_U])
                red(V, COL(19), work[:, 96:352], ALU.add, [W_U], [W_COL])
                act(S, work[:, 96:352], SCR(512, 768), AF.Square, [W_SCR], [W_U])
                red(V, COL(20), work[:, 96:352], ALU.add, [W_U], [W_COL])
                act(S, work[:, 96:352], SCR(768, 1024), AF.Square, [W_SCR], [W_U])
                red(V, COL(21), work[:, 96:352], ALU.add, [W_U], [W_COL])
                tt(V, COL(18), COL(18), COL(19), ALU.add, [W_COL], [W_COL])
                tt(V, COL(20), COL(20), COL(21), ALU.add, [W_COL], [W_COL])
                tt(V, COL(18), COL(18), COL(20), ALU.add, [W_COL], [W_COL])
                tsc(V, COL(19), COL(17), 1.0 / D, ALU.mult, [W_COL], [W_COL])
                tsc(V, COL(20), COL(18), 1.0 / D, ALU.mult, [W_COL], [W_COL])
                tt(V, COL(21), COL(19), COL(19), ALU.mult, [W_COL], [W_COL])
                tt(V, COL(22), COL(20), COL(21), ALU.subtract, [W_COL], [W_COL])
                act(S, COL(23), COL(22), AF.Sqrt, [W_COL, "c_eps5"], [W_COL], bias=consts[:, 78:79])
                P.op(V, lambda e: e.reciprocal(out=COL(23), in_=COL(23)),
                     [W_COL], [W_COL])
                sttop(V, COL(24), COL(19), -1.0, COL(23), ALU.mult, ALU.mult,
                      [W_COL], [W_COL])
                act(S, SCR(0, 1024), SCR(0, 1024), AF.Identity, [W_SCR, W_COL],
                    [W_SCR], bias=COL(24), scale=COL(23))
                if debug:
                    dmaop(DG, dbg["lnffn_dbg"][c * 128:(c + 1) * 128, :],
                          SCR(0, 1024), [W_SCR], [W_SCR])
                tt(V, work[:, 96:352] if False else ln2w[:, :],
                   x1_sb[:, c * D:(c + 1) * D], SCR(0, 1024), ALU.add,
                   ["x1r", W_SCR], ["ln2w"])
                dmaop(DG, out_d[c * 128:(c + 1) * 128, :], ln2w[:, :], ["ln2w"],
                      ["outd"])
        final_dg = P.cnt[DG]
        final_dsy = P.cnt[DSY]

    except _StopBuild:
        pass

    final_dg = P.cnt[DG]
    final_dsy = P.cnt[DSY]
    sems = {tag: nc.semaphore("sem_" + tag).__enter__()
            for tag in [T, V, S, G, SY, DG, DSY]}
    with nc.Block() as block:
        @block.tensor
        def _(e):
            for waits, fn, tag, inc in P.items[T]:
                for dep_tag, val in waits:
                    e.wait_ge(sems[dep_tag], val)
                res = fn(e)
                if res is not None:
                    res.then_inc(sems[tag], inc)
                else:
                    e.sem_inc(sems[tag], inc)

        @block.vector
        def _(e):
            for waits, fn, tag, inc in P.items[V]:
                for dep_tag, val in waits:
                    e.wait_ge(sems[dep_tag], val)
                res = fn(e)
                if res is not None:
                    res.then_inc(sems[tag], inc)
                else:
                    e.sem_inc(sems[tag], inc)

        @block.scalar
        def _(e):
            for waits, fn, tag, inc in P.items[S]:
                for dep_tag, val in waits:
                    e.wait_ge(sems[dep_tag], val)
                res = fn(e)
                if res is not None:
                    res.then_inc(sems[tag], inc)
                else:
                    e.sem_inc(sems[tag], inc)

        @block.gpsimd
        def _(e):
            for waits, fn, tag, inc in P.items[G]:
                for dep_tag, val in waits:
                    e.wait_ge(sems[dep_tag], val)
                res = fn(e)
                if res is not None:
                    res.then_inc(sems[tag], inc)
                else:
                    e.sem_inc(sems[tag], inc)
            e.wait_ge(sems[DG], final_dg)

        @block.sync
        def _(e):
            for waits, fn, tag, inc in P.items[SY]:
                for dep_tag, val in waits:
                    e.wait_ge(sems[dep_tag], val)
                res = fn(e)
                if res is not None:
                    res.then_inc(sems[tag], inc)
                else:
                    e.sem_inc(sems[tag], inc)
            e.wait_ge(sems[DSY], final_dsy)

    return nc


_NC_CACHE = {}


def _get_nc(TOK, debug):
    key = (TOK, debug)
    if key not in _NC_CACHE:
        _NC_CACHE[key] = build_nc(TOK, debug)
    return _NC_CACHE[key]


def make_inputs(inputs, TOK=2048, debug=False):
    f32 = np.float32
    hs = np.asarray(inputs["L0_attn_hs"], f32)[0]
    L = hs.shape[0]
    assert L == NCORES * TOK
    import ml_dtypes
    bf16 = ml_dtypes.bfloat16
    def pcl(a):
        a = np.asarray(a, f32)
        K = a.shape[1]
        return np.ascontiguousarray(
            a.reshape(8, 128, K).transpose(1, 0, 2).reshape(128, 8 * K))

    base = {
        "Wq": np.asarray(inputs["L0_attn_Wq"], f32),
        "Wk": np.asarray(inputs["L0_attn_Wk"], f32),
        "Wv": np.asarray(inputs["L0_attn_Wv"], f32),
        "Wid": np.asarray(inputs["L0_attn_Wid"], f32),
        "Wo": np.asarray(inputs["L0_attn_Wo"], f32),
        "Wb": pcl(inputs["L0_attn_Wb"]),
        "gW1": np.ascontiguousarray(np.asarray(inputs["L0_attn_gW1"], f32)[:D]),
        "gW1s": np.ascontiguousarray(np.asarray(inputs["L0_attn_gW1"], f32)[D:]),
        "gb1": pcl(np.asarray(inputs["L0_attn_gb1"], f32).reshape(D, 1)),
        "gW2": pcl(inputs["L0_attn_gW2"]),
        "cwq": pcl(inputs["L0_attn_cwq"]),
        "cwk": pcl(inputs["L0_attn_cwk"]),
        "cwv": pcl(inputs["L0_attn_cwv"]),
        "firs": pcl(np.asarray(inputs["L0_attn_firs"], f32).reshape(D, 3)),
        "firl": pcl(np.asarray(inputs["L0_attn_firl"], f32).reshape(D, 63)),
        "onw": np.ascontiguousarray(
            np.asarray(inputs["L0_attn_o_norm_w"], f32).reshape(2, 128).T),
        "wg": np.asarray(inputs["L0_ffn_w_gate"], f32),
        "wu": np.asarray(inputs["L0_ffn_w_up"], f32),
        "wd": np.asarray(inputs["L0_ffn_w_down"], f32),
        "ident": np.eye(128, dtype=bf16),
        "ident32": np.eye(128, dtype=f32),
        "ones": np.ones((128, 128), dtype=bf16),
        "mskSLn": (-np.tril(np.ones((128, 128), f32), -1)).astype(f32),
        "mskSUn": (-np.triu(np.ones((128, 128), f32), 1)).astype(f32),
        "mskIL": np.triu(np.ones((128, 128), f32), 0).astype(f32),
    }
    smalls = np.zeros((NH, 16), f32)
    smalls[:, 0:4] = np.asarray(inputs["L0_attn_log_temp"], f32)
    smalls[:, 4:8] = np.asarray(inputs["L0_attn_floor_param"], f32)
    smalls[:, 8] = np.asarray(inputs["L0_attn_conv_res_logit"], f32)
    smalls[:, 9] = np.asarray(inputs["L0_attn_alpha_id"], f32)
    smalls[:, 10:14] = np.asarray(inputs["L0_attn_gb2"], f32)[None, :]
    base["smalls"] = smalls
    in_maps = []
    for r in range(NCORES):
        m = dict(base)
        hsr = np.zeros((TOK + HAL, D), f32)
        lo = r * TOK - HAL
        src_lo = max(lo, 0)
        hsr[src_lo - lo:] = hs[src_lo: r * TOK + TOK]
        m["hs"] = hsr
        selr = np.zeros((128, 8), f32)
        if r > 0:
            selr[:, r - 1] = 1.0
        m["sel"] = selr
        in_maps.append(m)
    return in_maps


def kernel(**inputs):
    TOK = np.asarray(inputs["L0_attn_hs"]).shape[1] // NCORES
    nc_prog = _get_nc(TOK, False)
    in_maps = make_inputs(inputs, TOK)
    res = run_bass_kernel_spmd(nc_prog, in_maps, core_ids=list(range(NCORES)))
    outs = [np.asarray(r["out"]) for r in res.results]
    full = np.concatenate(outs, axis=0)[None]
    return full.astype(np.float32)
